# version 07
# baseline (speedup 1.0000x reference)
import math
import numpy as np
import concourse.bass as bass
import concourse.mybir as mybir
from concourse.bass_utils import run_bass_kernel_spmd

F32 = mybir.dt.float32
BF16 = mybir.dt.bfloat16
I32 = mybir.dt.int32
AF = mybir.ActivationFunctionType
ALU = mybir.AluOpType
f32 = np.float32


class Cfg:
    def __init__(self, S=2048, H=2048, NH=32, NKV=8, D=64, I=8192,
                 theta=10000.0, eps=1e-5, shift=12.0):
        self.S, self.H, self.NH, self.NKV, self.D, self.I = S, H, NH, NKV, D, I
        self.theta, self.eps, self.shift = theta, eps, shift
        self.NC = H // 128
        self.KC = NKV * D // 128
        self.QT = S // 512
        self.KT = S // 128
        self.ICN = I // 128
        self.GRP = NH // NKV
        assert D == 64 and H % 512 == 0 and S % 512 == 0 and I % 512 == 0
        assert self.ICN * 512 <= self.NC * S


FULL = Cfg()


def tile_lhsT(w):
    O, K = w.shape
    t = w.reshape(O // 128, 128, K // 128, 128).transpose(0, 3, 2, 1)
    return np.ascontiguousarray(t).reshape(O, K)


def tile_lhsT_pairs(w):
    O, K = w.shape
    t = tile_lhsT(w).reshape(O // 256, 2, 128, K).transpose(0, 2, 1, 3)
    return np.ascontiguousarray(t).reshape(O // 2, 2 * K)


def rope_tables(cfg):
    S, D = cfg.S, cfg.D
    inv = 1.0 / cfg.theta ** (np.arange(0, D, 2, dtype=np.float32) / D)
    pos = np.arange(S, dtype=np.float32)
    fr = np.outer(pos, inv).astype(np.float32)
    cos, sin = np.cos(fr), np.sin(fr)
    d = np.arange(128) % D
    cos2 = cos.T[d % (D // 2)]
    sgn = np.where(d < (D // 2), -1.0, 1.0).astype(np.float32)
    sin2 = sin.T[d % (D // 2)] * sgn[:, None]
    return np.ascontiguousarray(cos2), np.ascontiguousarray(sin2)


def build(cfg, dbg=False):
    nc = bass.Bass("TRN2", target_bir_lowering=False, detect_race_conditions=False)
    S, H, NC, KC, QT, KT, ICN = cfg.S, cfg.H, cfg.NC, cfg.KC, cfg.QT, cfg.KT, cfg.ICN
    NH, NKV, D, I = cfg.NH, cfg.NKV, cfg.D, cfg.I
    VW = NKV * (D + 1)
    NDU = 2 * (I // 4096)
    NDU = 2
    ICH = ICN // 2

    xT = nc.dram_tensor("xT", [H, S], F32, kind="ExternalInput").ap()
    wq_t = nc.dram_tensor("wq_t", [H, H], F32, kind="ExternalInput").ap()
    wk_t = nc.dram_tensor("wk_t", [NKV * D, H], F32, kind="ExternalInput").ap()
    wv_t = nc.dram_tensor("wv_t", [NKV * D, H], F32, kind="ExternalInput").ap()
    wo_t = nc.dram_tensor("wo_t", [H, H], F32, kind="ExternalInput").ap()
    gate_t = nc.dram_tensor("gate_t", [I, H], I32, kind="ExternalInput").ap()
    up_t = nc.dram_tensor("up_t", [I, H], I32, kind="ExternalInput").ap()
    down_t = nc.dram_tensor("down_t", [H, I], I32, kind="ExternalInput").ap()
    gsc = nc.dram_tensor("gsc", [128, ICN], F32, kind="ExternalInput").ap()
    usc = nc.dram_tensor("usc", [128, ICN], F32, kind="ExternalInput").ap()
    dsc = nc.dram_tensor("dsc", [128, NC], F32, kind="ExternalInput").ap()
    cosd = nc.dram_tensor("cosd", [128, S], F32, kind="ExternalInput").ap()
    sind = nc.dram_tensor("sind", [128, S], F32, kind="ExternalInput").ap()
    xo = nc.dram_tensor("xo", [H, S], F32, kind="ExternalOutput").ap()
    rowscr = nc.dram_tensor("rowscr", [4, 512], F32).ap()
    x1sp = nc.dram_tensor("x1sp", [H, S], BF16).ap()
    dbg_t = {}
    if dbg:
        for n, sh in [("d_xb", [128, NC * S]), ("d_q", [128, NC * S]),
                      ("d_kd", [128, NKV * S]), ("d_v", [128, KT * VW]),
                      ("d_att", [128, NC * S]), ("d_x1", [128, NC * S]),
                      ("d_h", [128, NC * S]), ("d_y2", [128, NC * S])]:
            dbg_t[n] = nc.dram_tensor(n, sh, F32, kind="ExternalOutput").ap()

    def rup(v):
        return (v + 31) & ~31

    sizes = [NC * S * 2, NC * S * 2, NKV * S * 2, KT * VW * 2, S * 2, S * 2,
             2 * H * 2, 2048 * 2, 2048, 2048, S * 2, 32, 256, ICN * 4, ICN * 4,
             NC * 4]
    mapsz = sum(rup(z) for z in sizes)
    resv = nc.alloc_sbuf_tensor("mapresv", [128, rup(mapsz)], mybir.dt.uint8)
    base = nc.lookup_mloc(resv).addr
    assert base % 32 == 0, base

    def at(name, shape, dt, off):
        assert off % 32 == 0, (name, off)
        return nc.alloc_sbuf_tensor_at(name, shape, dt, offset=base + off)

    o = 0
    arenaA = at("arenaA", [128, NC * S], BF16, o); o = rup(o + NC * S * 2)
    arenaB = at("arenaB", [128, NC * S], BF16, o); o = rup(o + NC * S * 2)
    kd_off = o
    Kdup = at("Kdup", [128, NKV * S], BF16, o); o = rup(o + NKV * S * 2)
    vs_off = o
    V_sb = at("V_sb", [128, KT * VW], BF16, o); o = rup(o + KT * VW * 2)
    cs_off = o
    cos2 = at("cos2", [128, S], BF16, o); o = rup(o + S * 2)
    sinsg = at("sinsg", [128, S], BF16, o); o = rup(o + S * 2)
    _wstr_off = o
    wstr = [at("wstr0", [128, H], BF16, o), at("wstr1", [128, H], BF16, o + H * 2)]
    o = rup(o + 2 * H * 2)
    ptile = at("ptile", [128, 2048], BF16, o); o = rup(o + 2048 * 2)
    tmpa = at("tmpa", [128, 512], F32, o); o = rup(o + 2048)
    tmpb = at("tmpb", [128, 512], F32, o); o = rup(o + 2048)
    bfpair = at("bfpair", [128, S], BF16, o); o = rup(o + S * 2)
    ones1 = at("ones1", [128, 1], BF16, o); o = rup(o + 32)
    ones2 = at("ones2", [34, 128], BF16, o); o = rup(o + 256)
    gscale = at("gscale", [128, ICN], F32, o); o = rup(o + ICN * 4)
    uscale = at("uscale", [128, ICN], F32, o); o = rup(o + ICN * 4)
    dscale = at("dscale", [128, NC], F32, o); o = rup(o + NC * 4)
    assert o == mapsz and o <= 212000, f"sbuf budget: {o} vs {mapsz}"
    xres = [at("xres0", [128, S], F32, kd_off),
            at("xres1", [128, S], F32, kd_off + S * 4)]
    wstr0_off = None
    TT2 = S // 1024
    harena = at("harena", [128, 2 * NC * S], BF16, 0)
    if NC * 1024 * 2 <= NKV * S * 2:
        x1t = at("x1t", [128, NC * 1024], BF16, kd_off)
    else:
        x1t = nc.alloc_sbuf_tensor("x1t", [128, NC * 1024], BF16)
    if 4 * 4096 <= KT * VW * 2:
        gbuf = [at("gbuf0", [128, H], BF16, vs_off),
                at("gbuf1", [128, H], BF16, vs_off + H * 2)]
        ubuf = [at("ubuf0", [128, H], BF16, vs_off + 2 * H * 2),
                at("ubuf1", [128, H], BF16, vs_off + 3 * H * 2)]
    else:
        gbuf = [nc.alloc_sbuf_tensor("gbuf0", [128, H], BF16),
                nc.alloc_sbuf_tensor("gbuf1", [128, H], BF16)]
        ubuf = [nc.alloc_sbuf_tensor("ubuf0", [128, H], BF16),
                nc.alloc_sbuf_tensor("ubuf1", [128, H], BF16)]
    if 2 * H * 2 >= 8192:
        dbuf = [at("dbuf0", [128, 2048], BF16, _wstr_off),
                at("dbuf1", [128, 2048], BF16, _wstr_off + 4096),
                gbuf[0], gbuf[1]]
    else:
        dbuf = [nc.alloc_sbuf_tensor("dbuf0", [128, 2048], BF16),
                nc.alloc_sbuf_tensor("dbuf1", [128, 2048], BF16),
                gbuf[0], gbuf[1]]
    if 4 * S >= 8192:
        rowf = at("rowf", [128, 1024], F32, cs_off)
        outst = [at("outst0", [128, 512], F32, cs_off + 4096),
                 at("outst1", [128, 512], F32, cs_off + 6144)]
    else:
        rowf = nc.alloc_sbuf_tensor("rowf", [128, 1024], F32)
        outst = [nc.alloc_sbuf_tensor("outst0", [128, 512], F32),
                 nc.alloc_sbuf_tensor("outst1", [128, 512], F32)]

    psw = [nc.alloc_psum_tensor(f"psw{j}", [128, 1024], F32) for j in range(4)]

    class Bank:
        def __init__(self, t, c0):
            self.t, self.c0 = t, c0

        def __getitem__(self, idx):
            rows = idx[0] if isinstance(idx, tuple) else idx
            cols = idx[1] if isinstance(idx, tuple) else slice(None)
            a = self.c0 + (cols.start or 0)
            b = self.c0 + (512 if cols.stop is None else cols.stop)
            return self.t[rows, a:b]

    ps = [Bank(psw[j // 2], (j % 2) * 512) for j in range(8)]
    grp = [ps[0:4], ps[4:8]]

    sems = {n: nc.alloc_semaphore(n) for n in
            ["XB", "SCL", "W0", "W1", "U0", "U1", "D0", "D1", "X0", "X1",
             "S0", "S1", "RS0", "RS1", "RB0", "RB1", "X1S", "X1T",
             "MM", "EV", "AC"]}
    cnt = {n: 0 for n in sems}

    phases = []
    cur = {e: [] for e in "tvsgy"}

    def endphase():
        nonlocal cur
        phases.append(cur)
        cur = {e: [] for e in "tvsgy"}

    def dump(pairs):
        if not dbg:
            return
        for name, src, width in pairs:
            nslice = (width + 511) // 512
            for j in range(nslice):
                w = min(512, width - j * 512)
                E('v', lambda v, src=src, j=j, w=w: v.tensor_scalar(
                    tmpa[:, 0:w], src[:, j * 512:j * 512 + w], 1.0, None,
                    ALU.mult), inc=("EV", 1))
                W('y', "EV", cnt["EV"])
                E('y', lambda y, name=name, j=j, w=w: y.dma_start(
                    out=dbg_t[name][:, j * 512:j * 512 + w], in_=tmpa[:, 0:w]),
                    inc=("S0", 16))
                W('v', "S0", cnt["S0"])
        endphase()

    def W(eng, sem, val):
        if val and val > 0:
            cur[eng].append(("w", sem, val))

    def E(eng, fn, inc=None):
        if inc:
            cnt[inc[0]] += inc[1]
        cur[eng].append(("e", fn, inc))
        return cnt[inc[0]] if inc else None

    E('v', lambda v: v.memset(ones1[:], 1.0))
    E('v', lambda v: v.memset(ones2[:], 1.0))
    E('v', lambda v: v.memset(bfpair[0:2, :], 0.0))
    E('v', lambda v: v.memset(bfpair[32:34, :], 0.0))
    E('v', lambda v: v.memset(V_sb[:], 1.0), inc=("EV", 1))

    for hc in range(NC):
        E('g', lambda g, hc=hc: g.dma_start(
            out=arenaA[:, hc * S:(hc + 1) * S],
            in_=xT[hc * 128:(hc + 1) * 128, :]), inc=("XB", 16))
    E('g', lambda g: g.dma_start(out=cos2[:], in_=cosd), inc=("XB", 16))
    E('g', lambda g: g.dma_start(out=sinsg[:], in_=sind), inc=("XB", 16))
    xb_total = cnt["XB"]
    ld_xb = ld_cs = xb_total
    E('y', lambda y: y.dma_start(out=gscale[:], in_=gsc), inc=("SCL", 16))
    E('y', lambda y: y.dma_start(out=uscale[:], in_=usc), inc=("SCL", 16))
    E('y', lambda y: y.dma_start(out=dscale[:], in_=dsc), inc=("SCL", 16))
    scl_total = cnt["SCL"]

    units = ([("K", i) for i in range(KC)] + [("V", i) for i in range(KC)]
             + [("Q", i) for i in range(NC)])
    wsrc = {"K": wk_t, "V": wv_t, "Q": wq_t}
    unit_ld = [0] * len(units)
    unit_mm = [0] * len(units)
    unit_ev = [0] * len(units)

    def rope_emit(v, pbank, c0, c1, dsts):
        v.tensor_tensor(tmpa[:], pbank[:], cos2[:, c0:c1], ALU.mult)
        for b in (0, 64):
            v.tensor_tensor(tmpb[b:b + 32, :], pbank[b + 32:b + 64, :],
                            sinsg[b:b + 32, c0:c1], ALU.mult)
            v.tensor_tensor(tmpb[b + 32:b + 64, :], pbank[b:b + 32, :],
                            sinsg[b + 32:b + 64, c0:c1], ALU.mult)
        ins = None
        for dst, srow in dsts:
            if srow is None:
                ins = v.tensor_tensor(dst, tmpa[:], tmpb[:], ALU.add)
            else:
                ins = v.tensor_tensor(dst, tmpa[srow:srow + 64, :],
                                      tmpb[srow:srow + 64, :], ALU.add)
        return ins

    for u, (kind, idx) in enumerate(units):
        if u >= 2:
            W('g', "MM", unit_mm[u - 2])
        unit_ld[u] = E('g', lambda g, kind=kind, idx=idx, u=u: g.dma_start(
            out=wstr[u % 2][:], in_=wsrc[kind][idx * 128:(idx + 1) * 128, :]),
            inc=("W%d" % (u % 2), 16))

        W('t', "XB", xb_total)
        W('t', "W%d" % (u % 2), unit_ld[u])
        if u >= 2:
            W('t', "EV", unit_ev[u - 2])
        g4 = grp[u % 2]
        if kind in ("K", "Q"):
            mm_sl = []
            for sl in range(QT):
                for hc in range(NC):
                    E('t', lambda t, sl=sl, hc=hc, u=u, g4=g4: t.matmul(
                        g4[sl][:], wstr[u % 2][:, hc * 128:(hc + 1) * 128],
                        arenaA[:, hc * S + sl * 512: hc * S + (sl + 1) * 512],
                        start=(hc == 0), stop=(hc == NC - 1)),
                        inc=("MM", 1) if hc == NC - 1 else None)
                mm_sl.append(cnt["MM"])
            unit_mm[u] = cnt["MM"]
            for sl in range(QT):
                W('v', "MM", mm_sl[sl])
                if u == 0 and sl == 0:
                    W('v', "XB", xb_total)
                c0, c1 = sl * 512, (sl + 1) * 512
                if kind == "Q":
                    dsts = [(arenaB[:, idx * S + c0: idx * S + c1], None)]
                else:
                    dsts = []
                    for gg, srow in ((2 * idx, 0), (2 * idx + 1, 64)):
                        for drow in (0, 64):
                            dsts.append((Kdup[drow:drow + 64,
                                              gg * S + c0: gg * S + c1], srow))
                E('v', lambda v, pb=g4[sl], c0=c0, c1=c1, dsts=dsts:
                    rope_emit(v, pb, c0, c1, dsts),
                    inc=("EV", 1) if sl == QT - 1 else None)
            unit_ev[u] = cnt["EV"]
        else:
            mm_tt = []
            ev_base = cnt["EV"]
            for tt in range(KT):
                pb, reg = g4[tt % 4], (tt // 4) * 128
                if tt >= 4:
                    W('t', "EV", ev_base + tt - 3)
                for hc in range(NC):
                    E('t', lambda t, tt=tt, hc=hc, u=u, pb=pb, reg=reg: t.matmul(
                        pb[:, reg:reg + 128],
                        arenaA[:, hc * S + tt * 128: hc * S + (tt + 1) * 128],
                        wstr[u % 2][:, hc * 128:(hc + 1) * 128],
                        start=(hc == 0), stop=(hc == NC - 1)),
                        inc=("MM", 1) if hc == NC - 1 else None)
                mm_tt.append(cnt["MM"])
            unit_mm[u] = cnt["MM"]
            vev = [0] * KT
            for tt in range(KT):
                W('v', "MM", mm_tt[tt])
                pb, reg = g4[tt % 4], (tt // 4) * 128
                dst = bass.AP(V_sb, tt * VW + 2 * idx * (D + 1),
                              [[KT * VW, 128], [D + 1, 2], [1, D]])
                src = bass.AP(pb.t, pb.c0 + reg, [[1024, 128], [64, 2], [1, D]])
                vev[tt] = E('v', lambda v, dst=dst, src=src: v.tensor_scalar(
                    dst, src, 1.0, None, ALU.mult), inc=("EV", 1))
            assert vev[0] == ev_base + 1
            unit_ev[u] = cnt["EV"]
    endphase()
    dump([("d_xb", arenaA, NC * S), ("d_q", arenaB, NC * S),
          ("d_kd", Kdup, NKV * S), ("d_v", V_sb, KT * VW)])

    attn_evs = {}
    rs_cnt = {0: 0, 1: 0}
    rb_cnt = {0: 0, 1: 0}
    it = -1
    for c in range(NH // 2):
        g = (2 * c) // cfg.GRP
        for qt in range(QT):
            it += 1
            par = it % 2
            q0 = qt * 512
            prev_exp = cnt["AC"]
            pvb = (ps[4], ps[5]) if par == 0 else (ps[6], ps[7])
            smm = [0] * KT
            pvmm = [0] * KT

            def emit_S(kt, c=c, g=g, q0=q0, smm=smm):
                b0, b1 = ps[2 * (kt % 2)], ps[2 * (kt % 2) + 1]
                k0 = kt * 128
                E('t', lambda t, b0=b0, k0=k0, c=c, g=g, q0=q0: t.matmul(
                    b0[:], Kdup[0:64, g * S + k0: g * S + k0 + 128],
                    arenaB[0:64, c * S + q0: c * S + q0 + 512],
                    tile_position=(0, 0), start=True, stop=True))
                smm[kt] = E('t', lambda t, b1=b1, k0=k0, c=c, g=g, q0=q0: t.matmul(
                    b1[:], Kdup[64:128, g * S + k0: g * S + k0 + 128],
                    arenaB[64:128, c * S + q0: c * S + q0 + 512],
                    tile_position=(64, 0), start=True, stop=True), inc=("MM", 1))

            W('t', "AC", prev_exp)
            if it >= 2 and attn_evs.get(it - 2):
                W('t', "EV", attn_evs[it - 2])
            emit_S(0)
            emit_S(1)
            for kt in range(KT):
                W('t', "AC", prev_exp + (kt + 1))
                vs = V_sb[:, kt * VW + g * (D + 1): kt * VW + g * (D + 1) + D + 1]
                p0 = 2 * (kt % 2) * 512
                E('t', lambda t, vs=vs, p0=p0, kt=kt, pvb=pvb: t.matmul(
                    pvb[0][0:D + 1, :], vs, ptile[:, p0:p0 + 512],
                    start=(kt == 0), stop=(kt == KT - 1)))
                pvmm[kt] = E('t', lambda t, vs=vs, p0=p0, kt=kt, pvb=pvb: t.matmul(
                    pvb[1][0:D + 1, :], vs, ptile[:, p0 + 512:p0 + 1024],
                    start=(kt == 0), stop=(kt == KT - 1)), inc=("MM", 1))
                if kt + 2 < KT:
                    emit_S(kt + 2)
            pv_done = cnt["MM"]
            for kt in range(KT):
                W('s', "MM", smm[kt])
                if kt >= 2:
                    W('s', "MM", pvmm[kt - 2])
                p0 = 2 * (kt % 2) * 512
                E('s', lambda s, kt=kt, p0=p0: s.activation(
                    ptile[:, p0:p0 + 1024], psw[kt % 2][:, :], AF.Exp,
                    bias=f32(-cfg.shift), scale=f32(1.0 / math.sqrt(D))),
                    inc=("AC", 1))
            W('v', "MM", pv_done)
            if it >= 1:
                W('v', "RS%d" % ((it - 1) % 2), rs_cnt[(it - 1) % 2])
            E('v', lambda v, pvb=pvb: v.reciprocal(tmpb[0:1, :], pvb[0][D:D + 1, :]))
            rp = E('v', lambda v, pvb=pvb: v.reciprocal(
                tmpb[32:33, :], pvb[1][D:D + 1, :]), inc=("EV", 1))
            W('y', "EV", rp)
            if it >= 1:
                W('y', "EV", attn_evs[it - 1])
            r0 = 2 * par
            E('y', lambda y, r0=r0: y.dma_start(
                out=rowscr[r0:r0 + 1, :], in_=tmpb[0:1, :]), inc=("RS%d" % par, 16))
            rs_cnt[par] = cnt["RS%d" % par]
            E('y', lambda y, r0=r0: y.dma_start(
                out=rowscr[r0 + 1:r0 + 2, :], in_=tmpb[32:33, :]),
                inc=("RS%d" % par, 16))
            rs_cnt[par] = cnt["RS%d" % par]
            W('y', "RS%d" % par, rs_cnt[par])
            E('y', lambda y, r0=r0: y.dma_start(
                out=tmpa[0:64, :],
                in_=bass.AP(rowscr.tensor, r0 * 512, [[0, 64], [1, 512]])),
                inc=("RB%d" % par, 16))
            E('y', lambda y, r0=r0: y.dma_start(
                out=tmpa[64:128, :],
                in_=bass.AP(rowscr.tensor, (r0 + 1) * 512, [[0, 64], [1, 512]])),
                inc=("RB%d" % par, 16))
            rb_cnt[par] = cnt["RB%d" % par]
            W('v', "RB%d" % par, rb_cnt[par])
            co = c * S + q0
            E('v', lambda v, co=co, pvb=pvb: v.tensor_tensor(
                arenaA[0:64, co:co + 512], pvb[0][0:64, :], tmpa[0:64, :], ALU.mult))
            attn_evs[it] = E('v', lambda v, co=co, pvb=pvb: v.tensor_tensor(
                arenaA[64:128, co:co + 512], pvb[1][0:64, :], tmpa[64:128, :], ALU.mult),
                inc=("EV", 1))
    endphase()
    dump([("d_att", arenaA, NC * S)])

    wo_ld = [0] * NC
    wo_mm = [0] * NC
    y_ev = [0] * NC
    xres_ld = [0] * NC
    for oc in range(NC):
        if oc >= 2:
            W('g', "MM", wo_mm[oc - 2])
        wo_ld[oc] = E('g', lambda g, oc=oc: g.dma_start(
            out=wstr[oc % 2][:], in_=wo_t[oc * 128:(oc + 1) * 128, :]),
            inc=("W%d" % (oc % 2), 16))
        if oc >= 2:
            W('y', "EV", y_ev[oc - 2])
        xres_ld[oc] = E('y', lambda y, oc=oc: y.dma_start(
            out=xres[oc % 2][:], in_=xT[oc * 128:(oc + 1) * 128, :]),
            inc=("X%d" % (oc % 2), 16))

        W('t', "W%d" % (oc % 2), wo_ld[oc])
        if oc >= 2:
            W('t', "EV", y_ev[oc - 2])
        mm_sl = []
        for sl in range(QT):
            for hc in range(NC):
                E('t', lambda t, sl=sl, hc=hc, oc=oc: t.matmul(
                    grp[oc % 2][sl][:], wstr[oc % 2][:, hc * 128:(hc + 1) * 128],
                    arenaA[:, hc * S + sl * 512: hc * S + (sl + 1) * 512],
                    start=(hc == 0), stop=(hc == NC - 1)),
                    inc=("MM", 1) if hc == NC - 1 else None)
            mm_sl.append(cnt["MM"])
        wo_mm[oc] = cnt["MM"]
        for sl in range(QT):
            W('v', "MM", mm_sl[sl])
            W('v', "X%d" % (oc % 2), xres_ld[oc])
            E('v', lambda v, sl=sl, oc=oc: v.tensor_tensor(
                arenaB[:, oc * S + sl * 512: oc * S + (sl + 1) * 512],
                grp[oc % 2][sl][:], xres[oc % 2][:, sl * 512:(sl + 1) * 512], ALU.add),
                inc=("EV", 1) if sl == QT - 1 else None)
        y_ev[oc] = cnt["EV"]

    sqd = {}
    std = {}
    for oc in range(NC):
        for sl in range(QT):
            j = oc * QT + sl
            W('s', "EV", y_ev[oc])
            if j >= 2:
                W('s', "MM", std[j - 2])
            slot = (j % 2) * 512
            sqd[j] = E('s', lambda s, oc=oc, sl=sl, slot=slot: s.activation(
                ptile[:, slot:slot + 512],
                arenaB[:, oc * S + sl * 512: oc * S + (sl + 1) * 512], AF.Square),
                inc=("AC", 1))
            if j == 0:
                W('t', "EV", y_ev[NC - 1])
            W('t', "AC", sqd[j])
            E('t', lambda t, oc=oc, sl=sl: t.matmul(
                ps[sl][0:1, :], ones1[:],
                arenaB[:, oc * S + sl * 512: oc * S + (sl + 1) * 512],
                start=(oc == 0), stop=(oc == NC - 1)))
            std[j] = E('t', lambda t, oc=oc, sl=sl, slot=slot: t.matmul(
                ps[4 + sl][0:1, :], ones1[:], ptile[:, slot:slot + 512],
                start=(oc == 0), stop=(oc == NC - 1)), inc=("MM", 1))
    ln1_stats_mm = cnt["MM"]

    A, Bc, Cc, Dc = (slice(0, 256), slice(256, 512), slice(512, 768),
                     slice(768, 1024))

    def ln_rows(sl_s1, sl_s2, dst0, dst1):
        last = None
        for k in range(2):
            c0, c1 = k * 256, (k + 1) * 256
            E('v', lambda v, c0=c0, c1=c1: v.tensor_scalar(
                rowf[0:1, A], sl_s1[0:1, c0:c1], 1.0 / H, None, ALU.mult))
            E('v', lambda v, c0=c0, c1=c1: v.tensor_scalar(
                rowf[0:1, Bc], sl_s2[0:1, c0:c1], 1.0 / H, None, ALU.mult))
            E('v', lambda v: v.tensor_tensor(rowf[0:1, Cc], rowf[0:1, A],
                                             rowf[0:1, A], ALU.mult))
            evv = E('v', lambda v: v.tensor_tensor(rowf[0:1, Bc], rowf[0:1, Bc],
                                                   rowf[0:1, Cc], ALU.subtract),
                    inc=("EV", 1))
            W('s', "EV", evv)
            acs = E('s', lambda s: s.activation(rowf[0:1, Cc], rowf[0:1, Bc],
                                                AF.Sqrt, bias=f32(cfg.eps),
                                                scale=f32(1.0)), inc=("AC", 1))
            W('v', "AC", acs)
            E('v', lambda v: v.reciprocal(rowf[0:1, Dc], rowf[0:1, Cc]))
            E('v', lambda v: v.scalar_tensor_tensor(
                rowf[0:1, Cc], rowf[0:1, A], -1.0, rowf[0:1, Dc],
                ALU.mult, ALU.mult))
            E('v', lambda v, c0=c0, c1=c1: v.tensor_scalar(
                dst0[0:1, c0:c1], rowf[0:1, Dc], 1.0, None, ALU.mult))
            last = E('v', lambda v, c0=c0, c1=c1: v.tensor_scalar(
                dst1[0:1, c0:c1], rowf[0:1, Cc], 1.0, None, ALU.mult),
                inc=("EV", 1))
        return last

    W('v', "MM", ln1_stats_mm)
    for sl in range(QT):
        ln1_rows = ln_rows(ps[sl], ps[4 + sl],
                           bfpair[0:1, sl * 512:(sl + 1) * 512],
                           bfpair[32:33, sl * 512:(sl + 1) * 512])

    apply_ev = {}
    for sl in range(QT):
        W('t', "EV", ln1_rows)
        if sl >= 2:
            W('t', "EV", apply_ev[sl - 2])
        ja, jb = 2 * (sl % 2), 2 * (sl % 2) + 1
        E('t', lambda t, sl=sl, ja=ja: t.matmul(
            ps[ja][:], ones2[0:2, :], bfpair[0:2, sl * 512:(sl + 1) * 512],
            start=True, stop=True))
        bcmm = E('t', lambda t, sl=sl, jb=jb: t.matmul(
            ps[jb][:], ones2[32:34, :], bfpair[32:34, sl * 512:(sl + 1) * 512],
            start=True, stop=True), inc=("MM", 1))
        for oc in range(NC):
            W('v', "MM", bcmm)
            E('v', lambda v, oc=oc, sl=sl, ja=ja: v.tensor_tensor(
                tmpa[:], arenaB[:, oc * S + sl * 512: oc * S + (sl + 1) * 512],
                ps[ja][:], ALU.mult))
            E('v', lambda v, oc=oc, sl=sl, jb=jb: v.tensor_tensor(
                arenaB[:, oc * S + sl * 512: oc * S + (sl + 1) * 512],
                tmpa[:], ps[jb][:], ALU.add),
                inc=("EV", 1) if oc == NC - 1 else None)
        apply_ev[sl] = cnt["EV"]
    W('y', "EV", apply_ev[QT - 1])
    for oc in range(NC):
        E('y', lambda y, oc=oc: y.dma_start(
            out=x1sp[oc * 128:(oc + 1) * 128, :],
            in_=arenaB[:, oc * S:(oc + 1) * S]), inc=("X1S", 16))
    x1s_total = cnt["X1S"]
    endphase()
    dump([("d_x1", arenaB, NC * S)])

    gmm, umm, gde, ude, sil, hev = {}, {}, {}, {}, {}, {}
    dnde, dnmm, y2ev, sq2, st2 = {}, {}, {}, {}, {}
    stout = {}
    x1t_ld = {}
    rows2_prev = 0
    apply2_prev = 0
    dn_all = {}

    def gu_load(tt, ic):
        pi = ic % 2
        if ic >= 2:
            W('g', "MM", gmm[(tt, ic - 2)])
        elif tt > 0:
            W('g', "MM", dnmm[(tt - 1, NDU - 2 + ic)])
        gld = E('g', lambda g, ic=ic, pi=pi: g.dma_start(
            out=gbuf[pi][:], in_=gate_t[ic * 128:(ic + 1) * 128, :]),
            inc=("W%d" % pi, 16))
        if ic >= 2:
            W('g', "MM", umm[(tt, ic - 2)])
        elif tt > 0:
            W('g', "MM", umm[(tt - 1, ICN - 2 + ic)])
        uld = E('g', lambda g, ic=ic, pi=pi: g.dma_start(
            out=ubuf[pi][:], in_=up_t[ic * 128:(ic + 1) * 128, :]),
            inc=("U%d" % pi, 16))
        W('v', "W%d" % pi, gld)
        gde[(tt, ic)] = E('v', lambda v, pi=pi: v.tensor_scalar(
            gbuf[pi][:], gbuf[pi][:], -128.0, None, ALU.add), inc=("EV", 1))
        W('v', "U%d" % pi, uld)
        ude[(tt, ic)] = E('v', lambda v, pi=pi: v.tensor_scalar(
            ubuf[pi][:], ubuf[pi][:], -128.0, None, ALU.add), inc=("EV", 1))

    NDU = 4 * NC

    def dn_load(tt, du):
        hd, q = du // 4, du % 4
        bi = du % 4
        if du >= 4:
            W('g', "MM", dnmm[(tt, du - 4)])
        elif bi >= 2:
            W('g', "MM", gmm[(tt, ICN - 2 + (bi - 2))])
        elif tt > 0:
            W('g', "MM", dnmm[(tt - 1, NDU - 4 + bi)])
        dw = (ICN // 4) * 128
        dld = E('g', lambda g, hd=hd, q=q, bi=bi, dw=dw: g.dma_start(
            out=dbuf[bi][:, 0:dw],
            in_=down_t[hd * 128:(hd + 1) * 128, q * dw:(q + 1) * dw]),
            inc=("D%d" % (bi % 2), 16))
        W('v', "D%d" % (bi % 2), dld)
        dnde[(tt, du)] = E('v', lambda v, bi=bi, dw=dw: v.tensor_scalar(
            dbuf[bi][:, 0:dw], dbuf[bi][:, 0:dw], -128.0, None, ALU.add),
            inc=("EV", 1))

    W('s', "SCL", scl_total)
    W('v', "SCL", scl_total)
    for tt in range(TT2):
        t0 = tt * 1024
        if tt > 0:
            W('y', "EV", apply2_prev)
        for hc in range(NC):
            E('y', lambda y, hc=hc, t0=t0: y.dma_start(
                out=x1t[:, hc * 1024:(hc + 1) * 1024],
                in_=x1sp[hc * 128:(hc + 1) * 128, t0:t0 + 1024]), inc=("X1T", 16))
        x1t_ld[tt] = cnt["X1T"]
        gu_load(tt, 0)
        gu_load(tt, 1)
        for ic in range(ICN):
            ki = (tt, ic)
            pi = ic % 2
            W('t', "EV", gde[ki])
            if ic == 0:
                W('t', "X1T", x1t_ld[tt])
                if tt > 0:
                    W('t', "EV", rows2_prev)
            if ic >= 1:
                W('t', "AC", sil[(tt, ic - 1)])
            elif tt > 0:
                W('t', "AC", sil[(tt - 1, ICN - 1)])
            for hc in range(NC):
                for sl in range(2):
                    E('t', lambda t, ic=ic, hc=hc, sl=sl, t0=t0, pi=pi: t.matmul(
                        ps[sl][:], gbuf[pi][:, hc * 128:(hc + 1) * 128],
                        x1t[:, hc * 1024 + sl * 512: hc * 1024 + (sl + 1) * 512],
                        start=(hc == 0), stop=(hc == NC - 1)),
                        inc=("MM", 1) if hc == NC - 1 and sl == 1 else None)
            gmm[ki] = cnt["MM"]
            W('t', "EV", ude[ki])
            if ic >= 1:
                W('t', "EV", hev[(tt, ic - 1)])
            elif tt > 0:
                W('t', "EV", hev[(tt - 1, ICN - 1)])
            for hc in range(NC):
                for sl in range(2):
                    E('t', lambda t, ic=ic, hc=hc, sl=sl, t0=t0, pi=pi: t.matmul(
                        ps[2 + sl][:], ubuf[pi][:, hc * 128:(hc + 1) * 128],
                        x1t[:, hc * 1024 + sl * 512: hc * 1024 + (sl + 1) * 512],
                        start=(hc == 0), stop=(hc == NC - 1)),
                        inc=("MM", 1) if hc == NC - 1 and sl == 1 else None)
            umm[ki] = cnt["MM"]
            W('s', "MM", gmm[ki])
            if ic >= 1:
                W('s', "EV", hev[(tt, ic - 1)])
            elif tt > 0:
                W('s', "EV", hev[(tt - 1, ICN - 1)])
                W('s', "MM", st2[(tt - 1, NC - 1)])
            sslot = (ic % 2) * 1024
            E('s', lambda s, ic=ic, sslot=sslot: s.activation(
                ptile[:, sslot:sslot + 512], ps[0][:], AF.Silu,
                bias=f32(0.0), scale=gscale[:, ic:ic + 1]))
            sil[ki] = E('s', lambda s, ic=ic, sslot=sslot: s.activation(
                ptile[:, sslot + 512:sslot + 1024], ps[1][:], AF.Silu,
                bias=f32(0.0), scale=gscale[:, ic:ic + 1]), inc=("AC", 1))
            if ic == 0:
                if tt == 0:
                    W('v', "X1S", x1s_total)
                else:
                    W('v', "MM", dn_all[tt - 1])
            W('v', "MM", umm[ki])
            W('v', "AC", sil[ki])
            E('v', lambda v, ic=ic, sslot=sslot: v.scalar_tensor_tensor(
                harena[:, ic * 1024: ic * 1024 + 512], ps[2][:],
                uscale[:, ic:ic + 1], ptile[:, sslot:sslot + 512],
                ALU.mult, ALU.mult))
            hev[ki] = E('v', lambda v, ic=ic, sslot=sslot: v.scalar_tensor_tensor(
                harena[:, ic * 1024 + 512: ic * 1024 + 1024], ps[3][:],
                uscale[:, ic:ic + 1], ptile[:, sslot + 512:sslot + 1024],
                ALU.mult, ALU.mult), inc=("EV", 1))
            if ic + 2 < ICN:
                gu_load(tt, ic + 2)
        h_done = cnt["EV"]

        for _du in range(4):
            dn_load(tt, _du)
        for hd in range(NC):
            pdb = (ps[4], ps[5]) if hd % 2 == 0 else (ps[6], ps[7])
            if hd == 0:
                W('t', "EV", h_done)
                if tt > 0:
                    W('t', "EV", apply2_prev)
            if hd >= 2:
                W('t', "EV", y2ev[(tt, hd - 2)])
            dw = (ICN // 4) * 128
            nblk = ICN // 4
            for q in range(4):
                du = 4 * hd + q
                W('t', "EV", dnde[(tt, du)])
                for ici in range(nblk):
                    ic = q * nblk + ici
                    for sl in range(2):
                        E('t', lambda t, ic=ic, ici=ici, du=du, sl=sl, pdb=pdb:
                            t.matmul(
                                pdb[sl][:], dbuf[du % 4][:, ici * 128:(ici + 1) * 128],
                                harena[:, ic * 1024 + sl * 512:
                                       ic * 1024 + (sl + 1) * 512],
                                start=(ic == 0), stop=(ic == ICN - 1)),
                            inc=("MM", 1) if ici == nblk - 1 and sl == 1 else None)
                dnmm[(tt, du)] = cnt["MM"]
                if du + 4 < NDU:
                    dn_load(tt, du + 4)
            W('v', "MM", dnmm[(tt, 4 * hd + 3)])
            E('v', lambda v, hd=hd, pdb=pdb: v.scalar_tensor_tensor(
                x1t[:, hd * 1024: hd * 1024 + 512], pdb[0][:],
                dscale[:, hd:hd + 1], x1t[:, hd * 1024: hd * 1024 + 512],
                ALU.mult, ALU.add))
            y2ev[(tt, hd)] = E('v', lambda v, hd=hd, pdb=pdb: v.scalar_tensor_tensor(
                x1t[:, hd * 1024 + 512: hd * 1024 + 1024], pdb[1][:],
                dscale[:, hd:hd + 1], x1t[:, hd * 1024 + 512: hd * 1024 + 1024],
                ALU.mult, ALU.add), inc=("EV", 1))
            W('s', "EV", y2ev[(tt, hd)])
            if hd >= 2:
                W('s', "MM", st2[(tt, hd - 2)])
            slot2 = (hd % 2) * 1024
            E('s', lambda s, hd=hd, slot2=slot2: s.activation(
                ptile[:, slot2:slot2 + 512],
                x1t[:, hd * 1024: hd * 1024 + 512], AF.Square))
            sq2[(tt, hd)] = E('s', lambda s, hd=hd, slot2=slot2: s.activation(
                ptile[:, slot2 + 512:slot2 + 1024],
                x1t[:, hd * 1024 + 512: hd * 1024 + 1024], AF.Square),
                inc=("AC", 1))

            def emit_stats(sh, tt=tt):
                slot3 = (sh % 2) * 1024
                if sh == 0:
                    W('t', "AC", sil[(tt, ICN - 1)])
                    W('t', "EV", hev[(tt, ICN - 1)])
                W('t', "AC", sq2[(tt, sh)])
                E('t', lambda t, sh=sh: t.matmul(
                    ps[0][0:1, :], ones1[:], x1t[:, sh * 1024: sh * 1024 + 512],
                    start=(sh == 0), stop=(sh == NC - 1)))
                E('t', lambda t, sh=sh: t.matmul(
                    ps[1][0:1, :], ones1[:],
                    x1t[:, sh * 1024 + 512: sh * 1024 + 1024],
                    start=(sh == 0), stop=(sh == NC - 1)))
                E('t', lambda t, sh=sh, slot3=slot3: t.matmul(
                    ps[2][0:1, :], ones1[:], ptile[:, slot3:slot3 + 512],
                    start=(sh == 0), stop=(sh == NC - 1)))
                st2[(tt, sh)] = E('t', lambda t, sh=sh, slot3=slot3: t.matmul(
                    ps[3][0:1, :], ones1[:], ptile[:, slot3 + 512:slot3 + 1024],
                    start=(sh == 0), stop=(sh == NC - 1)), inc=("MM", 1))

            if hd >= 1:
                emit_stats(hd - 1)
            if hd == NC - 1:
                emit_stats(hd)
        dn_all[tt] = cnt["MM"]

        W('v', "MM", st2[(tt, NC - 1)])
        for half in range(2):
            hc0 = t0 + half * 512
            rows2_prev = ln_rows(ps[half], ps[2 + half],
                                 bfpair[0:1, hc0:hc0 + 512],
                                 bfpair[32:33, hc0:hc0 + 512])
            jA, jB = (4, 5) if half == 0 else (6, 7)
            W('t', "EV", rows2_prev)
            E('t', lambda t, hc0=hc0, jA=jA: t.matmul(
                ps[jA][:], ones2[0:2, :], bfpair[0:2, hc0:hc0 + 512],
                start=True, stop=True))
            bc2 = E('t', lambda t, hc0=hc0, jB=jB: t.matmul(
                ps[jB][:], ones2[32:34, :], bfpair[32:34, hc0:hc0 + 512],
                start=True, stop=True), inc=("MM", 1))
            for hd in range(NC):
                W('v', "MM", bc2)
                if hd >= 2:
                    W('v', "S%d" % (hd % 2), stout[(tt, half, hd - 2)])
                E('v', lambda v, hd=hd, half=half, jA=jA: v.tensor_tensor(
                    tmpa[:], x1t[:, hd * 1024 + half * 512:
                                 hd * 1024 + (half + 1) * 512],
                    ps[jA][:], ALU.mult))
                oev = E('v', lambda v, hd=hd, jB=jB: v.tensor_tensor(
                    outst[hd % 2][:], tmpa[:], ps[jB][:], ALU.add), inc=("EV", 1))
                W('y', "EV", oev)
                stout[(tt, half, hd)] = E('y', lambda y, hd=hd, hc0=hc0: y.dma_start(
                    out=xo[hd * 128:(hd + 1) * 128, hc0:hc0 + 512],
                    in_=outst[hd % 2][:]), inc=("S%d" % (hd % 2), 16))
        apply2_prev = cnt["EV"]
    W('y', "S0", cnt["S0"])
    W('y', "S1", cnt["S1"])
    endphase()
    dump([("d_h", harena, ICN * 1024), ("d_y2", x1t, NC * 1024)])

    engmap = {'t': 'tensor', 'v': 'vector', 's': 'scalar', 'g': 'gpsimd', 'y': 'sync'}
    with nc.allow_low_precision(reason="bf16 matmul operands with fp32 psum "
                                       "accumulation per problem tolerance"):
        for pi, ph in enumerate(phases):
            with nc.Block(f"ph{pi}") as block:
                for e in "tvsgy":
                    ops = ph[e]
                    if not ops:
                        continue

                    def runner(eng, ops=ops):
                        for kind, a, b in ops:
                            if kind == "w":
                                eng.wait_ge(sems[a], b)
                            else:
                                ins = a(eng)
                                if b is not None:
                                    ins.then_inc(sems[b[0]], b[1])
                    getattr(block, engmap[e])(runner)
    return nc


_CACHE = {}


def _prep(inputs, cfg):
    key = (id(inputs["L0_ffn_gate_w"]), id(inputs["L0_attn_wq"]))
    if _CACHE.get("key") == key:
        return _CACHE["val"]
    cos2, sin2 = rope_tables(cfg)
    common = dict(
        wq_t=tile_lhsT(np.asarray(inputs["L0_attn_wq"], np.float32)),
        wk_t=tile_lhsT(np.asarray(inputs["L0_attn_wk"], np.float32)),
        wv_t=tile_lhsT(np.asarray(inputs["L0_attn_wv"], np.float32)),
        wo_t=tile_lhsT(np.asarray(inputs["L0_attn_wo"], np.float32)),
        gate_t=tile_lhsT(np.asarray(inputs["L0_ffn_gate_w"], np.int32)),
        up_t=tile_lhsT(np.asarray(inputs["L0_ffn_up_w"], np.int32)),
        down_t=tile_lhsT(np.asarray(inputs["L0_ffn_down_w"], np.int32)),
        gsc=np.ascontiguousarray(np.asarray(inputs["L0_ffn_gate_scale"],
                                            np.float32).reshape(cfg.ICN, 128).T),
        usc=np.ascontiguousarray(np.asarray(inputs["L0_ffn_up_scale"],
                                            np.float32).reshape(cfg.ICN, 128).T),
        dsc=np.ascontiguousarray(np.asarray(inputs["L0_ffn_down_scale"],
                                            np.float32).reshape(cfg.NC, 128).T),
        cosd=cos2, sind=sin2,
    )
    _CACHE["key"] = key
    _CACHE["val"] = common
    return common


def kernel(**inputs):
    cfg = FULL
    x = np.asarray(inputs["L0_attn_hidden_states"], np.float32)
    B = x.shape[0]
    common = _prep(inputs, cfg)
    nc = build(cfg)
    in_maps = []
    for b in range(B):
        m = dict(common)
        m["xT"] = np.ascontiguousarray(x[b].T)
        in_maps.append(m)
    res = run_bass_kernel_spmd(nc, in_maps, core_ids=list(range(B)))
    out = np.stack([np.ascontiguousarray(np.asarray(r["xo"]).T)
                    for r in res.results], axis=0)
    return out


if __name__ == "__main__":
    build(FULL)
    print("build ok")


# version 08 (final)
# speedup vs baseline: 1.1863x; 1.1863x over previous
import math
import numpy as np
import concourse.bass as bass
import concourse.mybir as mybir
from concourse.bass_utils import run_bass_kernel_spmd

F32 = mybir.dt.float32
BF16 = mybir.dt.bfloat16
I32 = mybir.dt.int32
AF = mybir.ActivationFunctionType
ALU = mybir.AluOpType
f32 = np.float32


class Cfg:
    def __init__(self, S=2048, H=2048, NH=32, NKV=8, D=64, I=8192,
                 theta=10000.0, eps=1e-5, shift=12.0):
        self.S, self.H, self.NH, self.NKV, self.D, self.I = S, H, NH, NKV, D, I
        self.theta, self.eps, self.shift = theta, eps, shift
        self.NC = H // 128
        self.KC = NKV * D // 128
        self.QT = S // 512
        self.KT = S // 128
        self.ICN = I // 128
        self.GRP = NH // NKV
        assert D == 64 and H % 512 == 0 and S % 512 == 0 and I % 512 == 0
        assert self.ICN * 512 <= self.NC * S


FULL = Cfg()


def tile_lhsT(w):
    O, K = w.shape
    t = w.reshape(O // 128, 128, K // 128, 128).transpose(0, 3, 2, 1)
    return np.ascontiguousarray(t).reshape(O, K)


def tile_lhsT_pairs(w):
    O, K = w.shape
    t = tile_lhsT(w).reshape(O // 256, 2, 128, K).transpose(0, 2, 1, 3)
    return np.ascontiguousarray(t).reshape(O // 2, 2 * K)


def rope_tables(cfg):
    S, D = cfg.S, cfg.D
    inv = 1.0 / cfg.theta ** (np.arange(0, D, 2, dtype=np.float32) / D)
    pos = np.arange(S, dtype=np.float32)
    fr = np.outer(pos, inv).astype(np.float32)
    cos, sin = np.cos(fr), np.sin(fr)
    d = np.arange(128) % D
    cos2 = cos.T[d % (D // 2)]
    sgn = np.where(d < (D // 2), -1.0, 1.0).astype(np.float32)
    sin2 = sin.T[d % (D // 2)] * sgn[:, None]
    return np.ascontiguousarray(cos2), np.ascontiguousarray(sin2)


def build(cfg, dbg=False):
    nc = bass.Bass("TRN2", target_bir_lowering=False, detect_race_conditions=False)
    S, H, NC, KC, QT, KT, ICN = cfg.S, cfg.H, cfg.NC, cfg.KC, cfg.QT, cfg.KT, cfg.ICN
    NH, NKV, D, I = cfg.NH, cfg.NKV, cfg.D, cfg.I
    VW = NKV * (D + 1)
    NDU = 2 * (I // 4096)
    NDU = 2
    ICH = ICN // 2

    xT = nc.dram_tensor("xT", [H, S], F32, kind="ExternalInput").ap()
    wq_t = nc.dram_tensor("wq_t", [H, H], F32, kind="ExternalInput").ap()
    wk_t = nc.dram_tensor("wk_t", [NKV * D, H], F32, kind="ExternalInput").ap()
    wv_t = nc.dram_tensor("wv_t", [NKV * D, H], F32, kind="ExternalInput").ap()
    wo_t = nc.dram_tensor("wo_t", [H, H], F32, kind="ExternalInput").ap()
    gate_t = nc.dram_tensor("gate_t", [I, H], I32, kind="ExternalInput").ap()
    up_t = nc.dram_tensor("up_t", [I, H], I32, kind="ExternalInput").ap()
    down_t = nc.dram_tensor("down_t", [H, I], I32, kind="ExternalInput").ap()
    gsc = nc.dram_tensor("gsc", [128, ICN], F32, kind="ExternalInput").ap()
    usc = nc.dram_tensor("usc", [128, ICN], F32, kind="ExternalInput").ap()
    dsc = nc.dram_tensor("dsc", [128, NC], F32, kind="ExternalInput").ap()
    cosd = nc.dram_tensor("cosd", [128, S], F32, kind="ExternalInput").ap()
    sind = nc.dram_tensor("sind", [128, S], F32, kind="ExternalInput").ap()
    xo = nc.dram_tensor("xo", [H, S], F32, kind="ExternalOutput").ap()
    rowscr = nc.dram_tensor("rowscr", [4, 512], F32).ap()
    x1sp = nc.dram_tensor("x1sp", [H, S], BF16).ap()
    dbg_t = {}
    if dbg:
        for n, sh in [("d_xb", [128, NC * S]), ("d_q", [128, NC * S]),
                      ("d_kd", [128, NKV * S]), ("d_v", [128, KT * VW]),
                      ("d_att", [128, NC * S]), ("d_x1", [128, NC * S]),
                      ("d_h", [128, NC * S]), ("d_y2", [128, NC * S])]:
            dbg_t[n] = nc.dram_tensor(n, sh, F32, kind="ExternalOutput").ap()

    def rup(v):
        return (v + 31) & ~31

    sizes = [NC * S * 2, NC * S * 2, NKV * S * 2, KT * VW * 2, S * 2, S * 2,
             2 * H * 2, 2048 * 2, 2048, 2048, S * 2, 32, 256, ICN * 4, ICN * 4,
             NC * 4]
    mapsz = sum(rup(z) for z in sizes)
    resv = nc.alloc_sbuf_tensor("mapresv", [128, rup(mapsz)], mybir.dt.uint8)
    base = nc.lookup_mloc(resv).addr
    assert base % 32 == 0, base

    def at(name, shape, dt, off):
        assert off % 32 == 0, (name, off)
        return nc.alloc_sbuf_tensor_at(name, shape, dt, offset=base + off)

    o = 0
    arenaA = at("arenaA", [128, NC * S], BF16, o); o = rup(o + NC * S * 2)
    arenaB = at("arenaB", [128, NC * S], BF16, o); o = rup(o + NC * S * 2)
    kd_off = o
    Kdup = at("Kdup", [128, NKV * S], BF16, o); o = rup(o + NKV * S * 2)
    vs_off = o
    V_sb = at("V_sb", [128, KT * VW], BF16, o); o = rup(o + KT * VW * 2)
    cs_off = o
    cos2 = at("cos2", [128, S], BF16, o); o = rup(o + S * 2)
    sinsg = at("sinsg", [128, S], BF16, o); o = rup(o + S * 2)
    _wstr_off = o
    wstr = [at("wstr0", [128, H], BF16, o), at("wstr1", [128, H], BF16, o + H * 2)]
    o = rup(o + 2 * H * 2)
    ptile = at("ptile", [128, 2048], BF16, o); o = rup(o + 2048 * 2)
    tmpa = at("tmpa", [128, 512], F32, o); o = rup(o + 2048)
    tmpb = at("tmpb", [128, 512], F32, o); o = rup(o + 2048)
    bfpair = at("bfpair", [128, S], BF16, o); o = rup(o + S * 2)
    ones1 = at("ones1", [128, 1], BF16, o); o = rup(o + 32)
    ones2 = at("ones2", [34, 128], BF16, o); o = rup(o + 256)
    gscale = at("gscale", [128, ICN], F32, o); o = rup(o + ICN * 4)
    uscale = at("uscale", [128, ICN], F32, o); o = rup(o + ICN * 4)
    dscale = at("dscale", [128, NC], F32, o); o = rup(o + NC * 4)
    assert o == mapsz and o <= 212000, f"sbuf budget: {o} vs {mapsz}"
    xres = [at("xres0", [128, S], F32, kd_off),
            at("xres1", [128, S], F32, kd_off + S * 4)]
    wstr0_off = None
    TT2 = S // 1024
    harena = at("harena", [128, 2 * NC * S], BF16, 0)
    if NC * 1024 * 2 <= NKV * S * 2:
        x1t = at("x1t", [128, NC * 1024], BF16, kd_off)
    else:
        x1t = nc.alloc_sbuf_tensor("x1t", [128, NC * 1024], BF16)
    if 4 * 4096 <= KT * VW * 2:
        gbuf = [at("gbuf0", [128, H], BF16, vs_off),
                at("gbuf1", [128, H], BF16, vs_off + H * 2)]
        ubuf = [at("ubuf0", [128, H], BF16, vs_off + 2 * H * 2),
                at("ubuf1", [128, H], BF16, vs_off + 3 * H * 2)]
    else:
        gbuf = [nc.alloc_sbuf_tensor("gbuf0", [128, H], BF16),
                nc.alloc_sbuf_tensor("gbuf1", [128, H], BF16)]
        ubuf = [nc.alloc_sbuf_tensor("ubuf0", [128, H], BF16),
                nc.alloc_sbuf_tensor("ubuf1", [128, H], BF16)]
    if 2 * H * 2 >= 8192:
        dbuf = [at("dbuf0", [128, 2048], BF16, _wstr_off),
                at("dbuf1", [128, 2048], BF16, _wstr_off + 4096),
                gbuf[0], gbuf[1]]
    else:
        dbuf = [nc.alloc_sbuf_tensor("dbuf0", [128, 2048], BF16),
                nc.alloc_sbuf_tensor("dbuf1", [128, 2048], BF16),
                gbuf[0], gbuf[1]]
    if 4 * S >= 8192:
        rowf = at("rowf", [128, 1024], F32, cs_off)
        outst = [at("outst0", [128, 512], F32, cs_off + 4096),
                 at("outst1", [128, 512], F32, cs_off + 6144)]
    else:
        rowf = nc.alloc_sbuf_tensor("rowf", [128, 1024], F32)
        outst = [nc.alloc_sbuf_tensor("outst0", [128, 512], F32),
                 nc.alloc_sbuf_tensor("outst1", [128, 512], F32)]

    psw = [nc.alloc_psum_tensor(f"psw{j}", [128, 1024], F32) for j in range(4)]

    class Bank:
        def __init__(self, t, c0):
            self.t, self.c0 = t, c0

        def __getitem__(self, idx):
            rows = idx[0] if isinstance(idx, tuple) else idx
            cols = idx[1] if isinstance(idx, tuple) else slice(None)
            a = self.c0 + (cols.start or 0)
            b = self.c0 + (512 if cols.stop is None else cols.stop)
            return self.t[rows, a:b]

    ps = [Bank(psw[j // 2], (j % 2) * 512) for j in range(8)]
    grp = [ps[0:4], ps[4:8]]

    sems = {n: nc.alloc_semaphore(n) for n in
            ["XB", "SCL", "W0", "W1", "U0", "U1", "D0", "D1", "X0", "X1",
             "S0", "S1", "RS0", "RS1", "RB0", "RB1", "X1S", "X1T",
             "MM", "EV", "AC"]}
    cnt = {n: 0 for n in sems}

    phases = []
    cur = {e: [] for e in "tvsgy"}

    def endphase():
        nonlocal cur
        phases.append(cur)
        cur = {e: [] for e in "tvsgy"}

    def dump(pairs):
        if not dbg:
            return
        for name, src, width in pairs:
            nslice = (width + 511) // 512
            for j in range(nslice):
                w = min(512, width - j * 512)
                E('v', lambda v, src=src, j=j, w=w: v.tensor_scalar(
                    tmpa[:, 0:w], src[:, j * 512:j * 512 + w], 1.0, None,
                    ALU.mult), inc=("EV", 1))
                W('y', "EV", cnt["EV"])
                E('y', lambda y, name=name, j=j, w=w: y.dma_start(
                    out=dbg_t[name][:, j * 512:j * 512 + w], in_=tmpa[:, 0:w]),
                    inc=("S0", 16))
                W('v', "S0", cnt["S0"])
        endphase()

    def W(eng, sem, val):
        if val and val > 0:
            cur[eng].append(("w", sem, val))

    def E(eng, fn, inc=None):
        if inc:
            cnt[inc[0]] += inc[1]
        cur[eng].append(("e", fn, inc))
        return cnt[inc[0]] if inc else None

    E('v', lambda v: v.memset(ones1[:], 1.0))
    E('v', lambda v: v.memset(ones2[:], 1.0))
    E('v', lambda v: v.memset(bfpair[0:2, :], 0.0))
    E('v', lambda v: v.memset(bfpair[32:34, :], 0.0))
    E('v', lambda v: v.memset(V_sb[:], 1.0), inc=("EV", 1))

    for hc in range(NC):
        E('g', lambda g, hc=hc: g.dma_start(
            out=arenaA[:, hc * S:(hc + 1) * S],
            in_=xT[hc * 128:(hc + 1) * 128, :]), inc=("XB", 16))
    E('g', lambda g: g.dma_start(out=cos2[:], in_=cosd), inc=("XB", 16))
    E('g', lambda g: g.dma_start(out=sinsg[:], in_=sind), inc=("XB", 16))
    xb_total = cnt["XB"]
    ld_xb = ld_cs = xb_total
    E('y', lambda y: y.dma_start(out=gscale[:], in_=gsc), inc=("SCL", 16))
    E('y', lambda y: y.dma_start(out=uscale[:], in_=usc), inc=("SCL", 16))
    E('y', lambda y: y.dma_start(out=dscale[:], in_=dsc), inc=("SCL", 16))
    scl_total = cnt["SCL"]

    units = ([("K", i) for i in range(KC)] + [("V", i) for i in range(KC)]
             + [("Q", i) for i in range(NC)])
    wsrc = {"K": wk_t, "V": wv_t, "Q": wq_t}
    unit_ld = [0] * len(units)
    unit_mm = [0] * len(units)
    unit_ev = [0] * len(units)

    def rope_emit(v, pbank, c0, c1, dsts):
        v.tensor_tensor(tmpa[:], pbank[:], cos2[:, c0:c1], ALU.mult)
        for b in (0, 64):
            v.tensor_tensor(tmpb[b:b + 32, :], pbank[b + 32:b + 64, :],
                            sinsg[b:b + 32, c0:c1], ALU.mult)
            v.tensor_tensor(tmpb[b + 32:b + 64, :], pbank[b:b + 32, :],
                            sinsg[b + 32:b + 64, c0:c1], ALU.mult)
        ins = None
        for dst, srow in dsts:
            if srow is None:
                ins = v.tensor_tensor(dst, tmpa[:], tmpb[:], ALU.add)
            else:
                ins = v.tensor_tensor(dst, tmpa[srow:srow + 64, :],
                                      tmpb[srow:srow + 64, :], ALU.add)
        return ins

    for u, (kind, idx) in enumerate(units):
        if u >= 2:
            W('g', "MM", unit_mm[u - 2])
        unit_ld[u] = E('g', lambda g, kind=kind, idx=idx, u=u: g.dma_start(
            out=wstr[u % 2][:], in_=wsrc[kind][idx * 128:(idx + 1) * 128, :]),
            inc=("W%d" % (u % 2), 16))

        W('t', "XB", xb_total)
        W('t', "W%d" % (u % 2), unit_ld[u])
        if u >= 2:
            W('t', "EV", unit_ev[u - 2])
        g4 = grp[u % 2]
        if kind in ("K", "Q"):
            mm_sl = []
            for sl in range(QT):
                for hc in range(NC):
                    E('t', lambda t, sl=sl, hc=hc, u=u, g4=g4: t.matmul(
                        g4[sl][:], wstr[u % 2][:, hc * 128:(hc + 1) * 128],
                        arenaA[:, hc * S + sl * 512: hc * S + (sl + 1) * 512],
                        start=(hc == 0), stop=(hc == NC - 1)),
                        inc=("MM", 1) if hc == NC - 1 else None)
                mm_sl.append(cnt["MM"])
            unit_mm[u] = cnt["MM"]
            for sl in range(QT):
                W('v', "MM", mm_sl[sl])
                if u == 0 and sl == 0:
                    W('v', "XB", xb_total)
                c0, c1 = sl * 512, (sl + 1) * 512
                if kind == "Q":
                    dsts = [(arenaB[:, idx * S + c0: idx * S + c1], None)]
                else:
                    dsts = []
                    for gg, srow in ((2 * idx, 0), (2 * idx + 1, 64)):
                        for drow in (0, 64):
                            dsts.append((Kdup[drow:drow + 64,
                                              gg * S + c0: gg * S + c1], srow))
                E('v', lambda v, pb=g4[sl], c0=c0, c1=c1, dsts=dsts:
                    rope_emit(v, pb, c0, c1, dsts),
                    inc=("EV", 1) if sl == QT - 1 else None)
            unit_ev[u] = cnt["EV"]
        else:
            mm_tt = [0] * KT
            ev_base = cnt["EV"]
            for g4i in range(KT // 4):
                if g4i > 0:
                    W('t', "EV", ev_base + 4 * g4i)
                for hc in range(NC):
                    for b4 in range(4):
                        tt = 4 * g4i + b4
                        pb, reg = g4[b4], g4i * 128
                        E('t', lambda t, tt=tt, hc=hc, u=u, pb=pb, reg=reg:
                            t.matmul(
                                pb[:, reg:reg + 128],
                                arenaA[:, hc * S + tt * 128:
                                       hc * S + (tt + 1) * 128],
                                wstr[u % 2][:, hc * 128:(hc + 1) * 128],
                                start=(hc == 0), stop=(hc == NC - 1)),
                            inc=("MM", 1) if hc == NC - 1 and b4 == 3 else None)
                for b4 in range(4):
                    mm_tt[4 * g4i + b4] = cnt["MM"]
            unit_mm[u] = cnt["MM"]
            for tt in range(KT):
                W('v', "MM", mm_tt[tt])
                pb, reg = g4[tt % 4], (tt // 4) * 128
                dst = bass.AP(V_sb, tt * VW + 2 * idx * (D + 1),
                              [[KT * VW, 128], [D + 1, 2], [1, D]])
                src = bass.AP(pb.t, pb.c0 + reg, [[1024, 128], [64, 2], [1, D]])
                E('v', lambda v, dst=dst, src=src: v.tensor_scalar(
                    dst, src, 1.0, None, ALU.mult), inc=("EV", 1))
            unit_ev[u] = cnt["EV"]
    endphase()
    dump([("d_xb", arenaA, NC * S), ("d_q", arenaB, NC * S),
          ("d_kd", Kdup, NKV * S), ("d_v", V_sb, KT * VW)])

    attn_evs = {}
    rs_cnt = {0: 0, 1: 0}
    rb_cnt = {0: 0, 1: 0}
    it = -1
    for c in range(NH // 2):
        g = (2 * c) // cfg.GRP
        for qt in range(QT):
            it += 1
            par = it % 2
            q0 = qt * 512
            prev_exp = cnt["AC"]
            pvb = (ps[4], ps[5]) if par == 0 else (ps[6], ps[7])
            smm = [0] * KT
            pvmm = [0] * KT

            def emit_S(kt, c=c, g=g, q0=q0, smm=smm):
                b0, b1 = ps[2 * (kt % 2)], ps[2 * (kt % 2) + 1]
                k0 = kt * 128
                E('t', lambda t, b0=b0, k0=k0, c=c, g=g, q0=q0: t.matmul(
                    b0[:], Kdup[0:64, g * S + k0: g * S + k0 + 128],
                    arenaB[0:64, c * S + q0: c * S + q0 + 512],
                    tile_position=(0, 0), start=True, stop=True))
                smm[kt] = E('t', lambda t, b1=b1, k0=k0, c=c, g=g, q0=q0: t.matmul(
                    b1[:], Kdup[64:128, g * S + k0: g * S + k0 + 128],
                    arenaB[64:128, c * S + q0: c * S + q0 + 512],
                    tile_position=(64, 0), start=True, stop=True), inc=("MM", 1))

            W('t', "AC", prev_exp)
            if it >= 2 and attn_evs.get(it - 2):
                W('t', "EV", attn_evs[it - 2])
            emit_S(0)
            emit_S(1)
            for kt in range(KT):
                W('t', "AC", prev_exp + (kt + 1))
                vs = V_sb[:, kt * VW + g * (D + 1): kt * VW + g * (D + 1) + D + 1]
                p0 = 2 * (kt % 2) * 512
                E('t', lambda t, vs=vs, p0=p0, kt=kt, pvb=pvb: t.matmul(
                    pvb[0][0:D + 1, :], vs, ptile[:, p0:p0 + 512],
                    start=(kt == 0), stop=(kt == KT - 1)))
                pvmm[kt] = E('t', lambda t, vs=vs, p0=p0, kt=kt, pvb=pvb: t.matmul(
                    pvb[1][0:D + 1, :], vs, ptile[:, p0 + 512:p0 + 1024],
                    start=(kt == 0), stop=(kt == KT - 1)), inc=("MM", 1))
                if kt + 2 < KT:
                    emit_S(kt + 2)
            pv_done = cnt["MM"]
            for kt in range(KT):
                W('s', "MM", smm[kt])
                if kt >= 2:
                    W('s', "MM", pvmm[kt - 2])
                p0 = 2 * (kt % 2) * 512
                E('s', lambda s, kt=kt, p0=p0: s.activation(
                    ptile[:, p0:p0 + 1024], psw[kt % 2][:, :], AF.Exp,
                    bias=f32(-cfg.shift), scale=f32(1.0 / math.sqrt(D))),
                    inc=("AC", 1))
            W('v', "MM", pv_done)
            if it >= 1:
                W('v', "RS%d" % ((it - 1) % 2), rs_cnt[(it - 1) % 2])
            E('v', lambda v, pvb=pvb: v.reciprocal(tmpb[0:1, :], pvb[0][D:D + 1, :]))
            rp = E('v', lambda v, pvb=pvb: v.reciprocal(
                tmpb[32:33, :], pvb[1][D:D + 1, :]), inc=("EV", 1))
            W('y', "EV", rp)
            if it >= 1:
                W('y', "EV", attn_evs[it - 1])
            r0 = 2 * par
            E('y', lambda y, r0=r0: y.dma_start(
                out=rowscr[r0:r0 + 1, :], in_=tmpb[0:1, :]), inc=("RS%d" % par, 16))
            rs_cnt[par] = cnt["RS%d" % par]
            E('y', lambda y, r0=r0: y.dma_start(
                out=rowscr[r0 + 1:r0 + 2, :], in_=tmpb[32:33, :]),
                inc=("RS%d" % par, 16))
            rs_cnt[par] = cnt["RS%d" % par]
            W('y', "RS%d" % par, rs_cnt[par])
            E('y', lambda y, r0=r0: y.dma_start(
                out=tmpa[0:64, :],
                in_=bass.AP(rowscr.tensor, r0 * 512, [[0, 64], [1, 512]])),
                inc=("RB%d" % par, 16))
            E('y', lambda y, r0=r0: y.dma_start(
                out=tmpa[64:128, :],
                in_=bass.AP(rowscr.tensor, (r0 + 1) * 512, [[0, 64], [1, 512]])),
                inc=("RB%d" % par, 16))
            rb_cnt[par] = cnt["RB%d" % par]
            W('v', "RB%d" % par, rb_cnt[par])
            co = c * S + q0
            E('v', lambda v, co=co, pvb=pvb: v.tensor_tensor(
                arenaA[0:64, co:co + 512], pvb[0][0:64, :], tmpa[0:64, :], ALU.mult))
            attn_evs[it] = E('v', lambda v, co=co, pvb=pvb: v.tensor_tensor(
                arenaA[64:128, co:co + 512], pvb[1][0:64, :], tmpa[64:128, :], ALU.mult),
                inc=("EV", 1))
    endphase()
    dump([("d_att", arenaA, NC * S)])

    wo_ld = [0] * NC
    wo_mm = [0] * NC
    y_ev = [0] * NC
    xres_ld = [0] * NC
    for oc in range(NC):
        if oc >= 2:
            W('g', "MM", wo_mm[oc - 2])
        wo_ld[oc] = E('g', lambda g, oc=oc: g.dma_start(
            out=wstr[oc % 2][:], in_=wo_t[oc * 128:(oc + 1) * 128, :]),
            inc=("W%d" % (oc % 2), 16))
        if oc >= 2:
            W('y', "EV", y_ev[oc - 2])
        xres_ld[oc] = E('y', lambda y, oc=oc: y.dma_start(
            out=xres[oc % 2][:], in_=xT[oc * 128:(oc + 1) * 128, :]),
            inc=("X%d" % (oc % 2), 16))

        W('t', "W%d" % (oc % 2), wo_ld[oc])
        if oc >= 2:
            W('t', "EV", y_ev[oc - 2])
        mm_sl = []
        for sl in range(QT):
            for hc in range(NC):
                E('t', lambda t, sl=sl, hc=hc, oc=oc: t.matmul(
                    grp[oc % 2][sl][:], wstr[oc % 2][:, hc * 128:(hc + 1) * 128],
                    arenaA[:, hc * S + sl * 512: hc * S + (sl + 1) * 512],
                    start=(hc == 0), stop=(hc == NC - 1)),
                    inc=("MM", 1) if hc == NC - 1 else None)
            mm_sl.append(cnt["MM"])
        wo_mm[oc] = cnt["MM"]
        for sl in range(QT):
            W('v', "MM", mm_sl[sl])
            W('v', "X%d" % (oc % 2), xres_ld[oc])
            E('v', lambda v, sl=sl, oc=oc: v.tensor_tensor(
                arenaB[:, oc * S + sl * 512: oc * S + (sl + 1) * 512],
                grp[oc % 2][sl][:], xres[oc % 2][:, sl * 512:(sl + 1) * 512], ALU.add),
                inc=("EV", 1) if sl == QT - 1 else None)
        y_ev[oc] = cnt["EV"]

    sqd = {}
    std = {}
    for oc in range(NC):
        for sl in range(QT):
            j = oc * QT + sl
            W('s', "EV", y_ev[oc])
            if j >= 2:
                W('s', "MM", std[j - 2])
            slot = (j % 2) * 512
            sqd[j] = E('s', lambda s, oc=oc, sl=sl, slot=slot: s.activation(
                ptile[:, slot:slot + 512],
                arenaB[:, oc * S + sl * 512: oc * S + (sl + 1) * 512], AF.Square),
                inc=("AC", 1))
            if j == 0:
                W('t', "EV", y_ev[NC - 1])
            W('t', "AC", sqd[j])
            E('t', lambda t, oc=oc, sl=sl: t.matmul(
                ps[sl][0:1, :], ones1[:],
                arenaB[:, oc * S + sl * 512: oc * S + (sl + 1) * 512],
                start=(oc == 0), stop=(oc == NC - 1)))
            std[j] = E('t', lambda t, oc=oc, sl=sl, slot=slot: t.matmul(
                ps[4 + sl][0:1, :], ones1[:], ptile[:, slot:slot + 512],
                start=(oc == 0), stop=(oc == NC - 1)), inc=("MM", 1))
    ln1_stats_mm = cnt["MM"]

    A, Bc, Cc, Dc = (slice(0, 256), slice(256, 512), slice(512, 768),
                     slice(768, 1024))

    def ln_rows(sl_s1, sl_s2, dst0, dst1):
        last = None
        for k in range(2):
            c0, c1 = k * 256, (k + 1) * 256
            E('v', lambda v, c0=c0, c1=c1: v.tensor_scalar(
                rowf[0:1, A], sl_s1[0:1, c0:c1], 1.0 / H, None, ALU.mult))
            E('v', lambda v, c0=c0, c1=c1: v.tensor_scalar(
                rowf[0:1, Bc], sl_s2[0:1, c0:c1], 1.0 / H, None, ALU.mult))
            E('v', lambda v: v.tensor_tensor(rowf[0:1, Cc], rowf[0:1, A],
                                             rowf[0:1, A], ALU.mult))
            evv = E('v', lambda v: v.tensor_tensor(rowf[0:1, Bc], rowf[0:1, Bc],
                                                   rowf[0:1, Cc], ALU.subtract),
                    inc=("EV", 1))
            W('s', "EV", evv)
            acs = E('s', lambda s: s.activation(rowf[0:1, Cc], rowf[0:1, Bc],
                                                AF.Sqrt, bias=f32(cfg.eps),
                                                scale=f32(1.0)), inc=("AC", 1))
            W('v', "AC", acs)
            E('v', lambda v: v.reciprocal(rowf[0:1, Dc], rowf[0:1, Cc]))
            E('v', lambda v: v.scalar_tensor_tensor(
                rowf[0:1, Cc], rowf[0:1, A], -1.0, rowf[0:1, Dc],
                ALU.mult, ALU.mult))
            E('v', lambda v, c0=c0, c1=c1: v.tensor_scalar(
                dst0[0:1, c0:c1], rowf[0:1, Dc], 1.0, None, ALU.mult))
            last = E('v', lambda v, c0=c0, c1=c1: v.tensor_scalar(
                dst1[0:1, c0:c1], rowf[0:1, Cc], 1.0, None, ALU.mult),
                inc=("EV", 1))
        return last

    W('v', "MM", ln1_stats_mm)
    for sl in range(QT):
        ln1_rows = ln_rows(ps[sl], ps[4 + sl],
                           bfpair[0:1, sl * 512:(sl + 1) * 512],
                           bfpair[32:33, sl * 512:(sl + 1) * 512])

    apply_ev = {}
    for sl in range(QT):
        W('t', "EV", ln1_rows)
        if sl >= 2:
            W('t', "EV", apply_ev[sl - 2])
        ja, jb = 2 * (sl % 2), 2 * (sl % 2) + 1
        E('t', lambda t, sl=sl, ja=ja: t.matmul(
            ps[ja][:], ones2[0:2, :], bfpair[0:2, sl * 512:(sl + 1) * 512],
            start=True, stop=True))
        bcmm = E('t', lambda t, sl=sl, jb=jb: t.matmul(
            ps[jb][:], ones2[32:34, :], bfpair[32:34, sl * 512:(sl + 1) * 512],
            start=True, stop=True), inc=("MM", 1))
        for oc in range(NC):
            W('v', "MM", bcmm)
            E('v', lambda v, oc=oc, sl=sl, ja=ja: v.tensor_tensor(
                tmpa[:], arenaB[:, oc * S + sl * 512: oc * S + (sl + 1) * 512],
                ps[ja][:], ALU.mult))
            E('v', lambda v, oc=oc, sl=sl, jb=jb: v.tensor_tensor(
                arenaB[:, oc * S + sl * 512: oc * S + (sl + 1) * 512],
                tmpa[:], ps[jb][:], ALU.add),
                inc=("EV", 1) if oc == NC - 1 else None)
        apply_ev[sl] = cnt["EV"]
    W('y', "EV", apply_ev[QT - 1])
    for oc in range(NC):
        E('y', lambda y, oc=oc: y.dma_start(
            out=x1sp[oc * 128:(oc + 1) * 128, :],
            in_=arenaB[:, oc * S:(oc + 1) * S]), inc=("X1S", 16))
    x1s_total = cnt["X1S"]
    endphase()
    dump([("d_x1", arenaB, NC * S)])

    gmm, umm, gde, ude, sil, hev = {}, {}, {}, {}, {}, {}
    dnde, dnmm, y2ev, sq2, st2 = {}, {}, {}, {}, {}
    stout = {}
    x1t_ld = {}
    rows2_prev = 0
    apply2_prev = 0
    dn_all = {}

    def gu_load(tt, ic):
        pi = ic % 2
        if ic >= 2:
            W('g', "MM", gmm[(tt, ic - 2)])
        elif tt > 0:
            W('g', "MM", dnmm[(tt - 1, NDU - 2 + ic)])
        gld = E('g', lambda g, ic=ic, pi=pi: g.dma_start(
            out=gbuf[pi][:], in_=gate_t[ic * 128:(ic + 1) * 128, :]),
            inc=("W%d" % pi, 16))
        if ic >= 2:
            W('g', "MM", umm[(tt, ic - 2)])
        elif tt > 0:
            W('g', "MM", umm[(tt - 1, ICN - 2 + ic)])
        uld = E('g', lambda g, ic=ic, pi=pi: g.dma_start(
            out=ubuf[pi][:], in_=up_t[ic * 128:(ic + 1) * 128, :]),
            inc=("U%d" % pi, 16))
        W('v', "W%d" % pi, gld)
        gde[(tt, ic)] = E('v', lambda v, pi=pi: v.tensor_scalar(
            gbuf[pi][:], gbuf[pi][:], -128.0, None, ALU.add), inc=("EV", 1))
        W('v', "U%d" % pi, uld)
        ude[(tt, ic)] = E('v', lambda v, pi=pi: v.tensor_scalar(
            ubuf[pi][:], ubuf[pi][:], -128.0, None, ALU.add), inc=("EV", 1))

    NDU = 4 * NC

    def dn_load(tt, du):
        hd, q = du // 4, du % 4
        bi = du % 4
        if du >= 4:
            W('g', "MM", dnmm[(tt, du - 4)])
        elif bi >= 2:
            W('g', "MM", gmm[(tt, ICN - 2 + (bi - 2))])
        elif tt > 0:
            W('g', "MM", dnmm[(tt - 1, NDU - 4 + bi)])
        dw = (ICN // 4) * 128
        dld = E('g', lambda g, hd=hd, q=q, bi=bi, dw=dw: g.dma_start(
            out=dbuf[bi][:, 0:dw],
            in_=down_t[hd * 128:(hd + 1) * 128, q * dw:(q + 1) * dw]),
            inc=("D%d" % (bi % 2), 16))
        W('v', "D%d" % (bi % 2), dld)
        dnde[(tt, du)] = E('v', lambda v, bi=bi, dw=dw: v.tensor_scalar(
            dbuf[bi][:, 0:dw], dbuf[bi][:, 0:dw], -128.0, None, ALU.add),
            inc=("EV", 1))

    W('s', "SCL", scl_total)
    W('v', "SCL", scl_total)
    for tt in range(TT2):
        t0 = tt * 1024
        if tt > 0:
            W('y', "EV", apply2_prev)
        for hc in range(NC):
            E('y', lambda y, hc=hc, t0=t0: y.dma_start(
                out=x1t[:, hc * 1024:(hc + 1) * 1024],
                in_=x1sp[hc * 128:(hc + 1) * 128, t0:t0 + 1024]), inc=("X1T", 16))
        x1t_ld[tt] = cnt["X1T"]
        gu_load(tt, 0)
        gu_load(tt, 1)
        for ic in range(ICN):
            ki = (tt, ic)
            pi = ic % 2
            W('t', "EV", gde[ki])
            if ic == 0:
                W('t', "X1T", x1t_ld[tt])
                if tt > 0:
                    W('t', "EV", rows2_prev)
            if ic >= 1:
                W('t', "AC", sil[(tt, ic - 1)])
            elif tt > 0:
                W('t', "AC", sil[(tt - 1, ICN - 1)])
            for hc in range(NC):
                for sl in range(2):
                    E('t', lambda t, ic=ic, hc=hc, sl=sl, t0=t0, pi=pi: t.matmul(
                        ps[sl][:], gbuf[pi][:, hc * 128:(hc + 1) * 128],
                        x1t[:, hc * 1024 + sl * 512: hc * 1024 + (sl + 1) * 512],
                        start=(hc == 0), stop=(hc == NC - 1)),
                        inc=("MM", 1) if hc == NC - 1 and sl == 1 else None)
            gmm[ki] = cnt["MM"]
            W('t', "EV", ude[ki])
            if ic >= 1:
                W('t', "EV", hev[(tt, ic - 1)])
            elif tt > 0:
                W('t', "EV", hev[(tt - 1, ICN - 1)])
            for hc in range(NC):
                for sl in range(2):
                    E('t', lambda t, ic=ic, hc=hc, sl=sl, t0=t0, pi=pi: t.matmul(
                        ps[2 + sl][:], ubuf[pi][:, hc * 128:(hc + 1) * 128],
                        x1t[:, hc * 1024 + sl * 512: hc * 1024 + (sl + 1) * 512],
                        start=(hc == 0), stop=(hc == NC - 1)),
                        inc=("MM", 1) if hc == NC - 1 and sl == 1 else None)
            umm[ki] = cnt["MM"]
            W('s', "MM", gmm[ki])
            if ic >= 1:
                W('s', "EV", hev[(tt, ic - 1)])
            elif tt > 0:
                W('s', "EV", hev[(tt - 1, ICN - 1)])
                W('s', "MM", st2[(tt - 1, NC - 1)])
            sslot = (ic % 2) * 1024
            E('s', lambda s, ic=ic, sslot=sslot: s.activation(
                ptile[:, sslot:sslot + 512], ps[0][:], AF.Silu,
                bias=f32(0.0), scale=gscale[:, ic:ic + 1]))
            sil[ki] = E('s', lambda s, ic=ic, sslot=sslot: s.activation(
                ptile[:, sslot + 512:sslot + 1024], ps[1][:], AF.Silu,
                bias=f32(0.0), scale=gscale[:, ic:ic + 1]), inc=("AC", 1))
            if ic == 0:
                if tt == 0:
                    W('v', "X1S", x1s_total)
                else:
                    W('v', "MM", dn_all[tt - 1])
            W('v', "MM", umm[ki])
            W('v', "AC", sil[ki])
            E('v', lambda v, ic=ic, sslot=sslot: v.scalar_tensor_tensor(
                harena[:, ic * 1024: ic * 1024 + 512], ps[2][:],
                uscale[:, ic:ic + 1], ptile[:, sslot:sslot + 512],
                ALU.mult, ALU.mult))
            hev[ki] = E('v', lambda v, ic=ic, sslot=sslot: v.scalar_tensor_tensor(
                harena[:, ic * 1024 + 512: ic * 1024 + 1024], ps[3][:],
                uscale[:, ic:ic + 1], ptile[:, sslot + 512:sslot + 1024],
                ALU.mult, ALU.mult), inc=("EV", 1))
            if ic + 2 < ICN:
                gu_load(tt, ic + 2)
        h_done = cnt["EV"]

        for _du in range(4):
            dn_load(tt, _du)
        for hd in range(NC):
            pdb = (ps[4], ps[5]) if hd % 2 == 0 else (ps[6], ps[7])
            if hd == 0:
                W('t', "EV", h_done)
                if tt > 0:
                    W('t', "EV", apply2_prev)
            if hd >= 2:
                W('t', "EV", y2ev[(tt, hd - 2)])
            dw = (ICN // 4) * 128
            nblk = ICN // 4
            for q in range(4):
                du = 4 * hd + q
                W('t', "EV", dnde[(tt, du)])
                for ici in range(nblk):
                    ic = q * nblk + ici
                    for sl in range(2):
                        E('t', lambda t, ic=ic, ici=ici, du=du, sl=sl, pdb=pdb:
                            t.matmul(
                                pdb[sl][:], dbuf[du % 4][:, ici * 128:(ici + 1) * 128],
                                harena[:, ic * 1024 + sl * 512:
                                       ic * 1024 + (sl + 1) * 512],
                                start=(ic == 0), stop=(ic == ICN - 1)),
                            inc=("MM", 1) if ici == nblk - 1 and sl == 1 else None)
                dnmm[(tt, du)] = cnt["MM"]
                if du + 4 < NDU:
                    dn_load(tt, du + 4)
            W('v', "MM", dnmm[(tt, 4 * hd + 3)])
            E('v', lambda v, hd=hd, pdb=pdb: v.scalar_tensor_tensor(
                x1t[:, hd * 1024: hd * 1024 + 512], pdb[0][:],
                dscale[:, hd:hd + 1], x1t[:, hd * 1024: hd * 1024 + 512],
                ALU.mult, ALU.add))
            y2ev[(tt, hd)] = E('v', lambda v, hd=hd, pdb=pdb: v.scalar_tensor_tensor(
                x1t[:, hd * 1024 + 512: hd * 1024 + 1024], pdb[1][:],
                dscale[:, hd:hd + 1], x1t[:, hd * 1024 + 512: hd * 1024 + 1024],
                ALU.mult, ALU.add), inc=("EV", 1))
            W('s', "EV", y2ev[(tt, hd)])
            if hd >= 2:
                W('s', "MM", st2[(tt, hd - 2)])
            slot2 = (hd % 2) * 1024
            E('s', lambda s, hd=hd, slot2=slot2: s.activation(
                ptile[:, slot2:slot2 + 512],
                x1t[:, hd * 1024: hd * 1024 + 512], AF.Square))
            sq2[(tt, hd)] = E('s', lambda s, hd=hd, slot2=slot2: s.activation(
                ptile[:, slot2 + 512:slot2 + 1024],
                x1t[:, hd * 1024 + 512: hd * 1024 + 1024], AF.Square),
                inc=("AC", 1))

            def emit_stats(sh, tt=tt):
                slot3 = (sh % 2) * 1024
                if sh == 0:
                    W('t', "AC", sil[(tt, ICN - 1)])
                    W('t', "EV", hev[(tt, ICN - 1)])
                W('t', "AC", sq2[(tt, sh)])
                E('t', lambda t, sh=sh: t.matmul(
                    ps[0][0:1, :], ones1[:], x1t[:, sh * 1024: sh * 1024 + 512],
                    start=(sh == 0), stop=(sh == NC - 1)))
                E('t', lambda t, sh=sh: t.matmul(
                    ps[1][0:1, :], ones1[:],
                    x1t[:, sh * 1024 + 512: sh * 1024 + 1024],
                    start=(sh == 0), stop=(sh == NC - 1)))
                E('t', lambda t, sh=sh, slot3=slot3: t.matmul(
                    ps[2][0:1, :], ones1[:], ptile[:, slot3:slot3 + 512],
                    start=(sh == 0), stop=(sh == NC - 1)))
                st2[(tt, sh)] = E('t', lambda t, sh=sh, slot3=slot3: t.matmul(
                    ps[3][0:1, :], ones1[:], ptile[:, slot3 + 512:slot3 + 1024],
                    start=(sh == 0), stop=(sh == NC - 1)), inc=("MM", 1))

            if hd >= 1:
                emit_stats(hd - 1)
            if hd == NC - 1:
                emit_stats(hd)
        dn_all[tt] = cnt["MM"]

        W('v', "MM", st2[(tt, NC - 1)])
        for half in range(2):
            hc0 = t0 + half * 512
            rows2_prev = ln_rows(ps[half], ps[2 + half],
                                 bfpair[0:1, hc0:hc0 + 512],
                                 bfpair[32:33, hc0:hc0 + 512])
            jA, jB = (4, 5) if half == 0 else (6, 7)
            W('t', "EV", rows2_prev)
            E('t', lambda t, hc0=hc0, jA=jA: t.matmul(
                ps[jA][:], ones2[0:2, :], bfpair[0:2, hc0:hc0 + 512],
                start=True, stop=True))
            bc2 = E('t', lambda t, hc0=hc0, jB=jB: t.matmul(
                ps[jB][:], ones2[32:34, :], bfpair[32:34, hc0:hc0 + 512],
                start=True, stop=True), inc=("MM", 1))
            for hd in range(NC):
                W('v', "MM", bc2)
                if hd >= 2:
                    W('v', "S%d" % (hd % 2), stout[(tt, half, hd - 2)])
                E('v', lambda v, hd=hd, half=half, jA=jA: v.tensor_tensor(
                    tmpa[:], x1t[:, hd * 1024 + half * 512:
                                 hd * 1024 + (half + 1) * 512],
                    ps[jA][:], ALU.mult))
                oev = E('v', lambda v, hd=hd, jB=jB: v.tensor_tensor(
                    outst[hd % 2][:], tmpa[:], ps[jB][:], ALU.add), inc=("EV", 1))
                W('y', "EV", oev)
                stout[(tt, half, hd)] = E('y', lambda y, hd=hd, hc0=hc0: y.dma_start(
                    out=xo[hd * 128:(hd + 1) * 128, hc0:hc0 + 512],
                    in_=outst[hd % 2][:]), inc=("S%d" % (hd % 2), 16))
        apply2_prev = cnt["EV"]
    W('y', "S0", cnt["S0"])
    W('y', "S1", cnt["S1"])
    endphase()
    dump([("d_h", harena, ICN * 1024), ("d_y2", x1t, NC * 1024)])

    engmap = {'t': 'tensor', 'v': 'vector', 's': 'scalar', 'g': 'gpsimd', 'y': 'sync'}
    with nc.allow_low_precision(reason="bf16 matmul operands with fp32 psum "
                                       "accumulation per problem tolerance"):
        for pi, ph in enumerate(phases):
            with nc.Block(f"ph{pi}") as block:
                for e in "tvsgy":
                    ops = ph[e]
                    if not ops:
                        continue

                    def runner(eng, ops=ops):
                        for kind, a, b in ops:
                            if kind == "w":
                                eng.wait_ge(sems[a], b)
                            else:
                                ins = a(eng)
                                if b is not None:
                                    ins.then_inc(sems[b[0]], b[1])
                    getattr(block, engmap[e])(runner)
    return nc


_CACHE = {}


def _prep(inputs, cfg):
    key = (id(inputs["L0_ffn_gate_w"]), id(inputs["L0_attn_wq"]))
    if _CACHE.get("key") == key:
        return _CACHE["val"]
    cos2, sin2 = rope_tables(cfg)
    common = dict(
        wq_t=tile_lhsT(np.asarray(inputs["L0_attn_wq"], np.float32)),
        wk_t=tile_lhsT(np.asarray(inputs["L0_attn_wk"], np.float32)),
        wv_t=tile_lhsT(np.asarray(inputs["L0_attn_wv"], np.float32)),
        wo_t=tile_lhsT(np.asarray(inputs["L0_attn_wo"], np.float32)),
        gate_t=tile_lhsT(np.asarray(inputs["L0_ffn_gate_w"], np.int32)),
        up_t=tile_lhsT(np.asarray(inputs["L0_ffn_up_w"], np.int32)),
        down_t=tile_lhsT(np.asarray(inputs["L0_ffn_down_w"], np.int32)),
        gsc=np.ascontiguousarray(np.asarray(inputs["L0_ffn_gate_scale"],
                                            np.float32).reshape(cfg.ICN, 128).T),
        usc=np.ascontiguousarray(np.asarray(inputs["L0_ffn_up_scale"],
                                            np.float32).reshape(cfg.ICN, 128).T),
        dsc=np.ascontiguousarray(np.asarray(inputs["L0_ffn_down_scale"],
                                            np.float32).reshape(cfg.NC, 128).T),
        cosd=cos2, sind=sin2,
    )
    _CACHE["key"] = key
    _CACHE["val"] = common
    return common


def kernel(**inputs):
    cfg = FULL
    x = np.asarray(inputs["L0_attn_hidden_states"], np.float32)
    B = x.shape[0]
    common = _prep(inputs, cfg)
    nc = build(cfg)
    in_maps = []
    for b in range(B):
        m = dict(common)
        m["xT"] = np.ascontiguousarray(x[b].T)
        in_maps.append(m)
    res = run_bass_kernel_spmd(nc, in_maps, core_ids=list(range(B)))
    out = np.stack([np.ascontiguousarray(np.asarray(r["xo"]).T)
                    for r in res.results], axis=0)
    return out


if __name__ == "__main__":
    build(FULL)
    print("build ok")
